# version 01
# baseline (speedup 1.0000x reference)
import numpy as np
import ml_dtypes
from contextlib import ExitStack

import concourse.bass as bass
import concourse.mybir as mybir
from concourse.bass_utils import run_bass_kernel_spmd

F32 = mybir.dt.float32
BF16 = mybir.dt.bfloat16
AF = mybir.ActivationFunctionType
ALU = mybir.AluOpType
AX = mybir.AxisListType

NCORES = 8
D = 2048
BATCH = 4
SEQ = 2048
T = 1024
DC = D // 128
TTI = T // 128


class Prog:
    ENGS = ("tensor", "vector", "scalar", "gpsimd", "sync")
    DMA_ENGS = ("gpsimd", "sync", "scalar")

    STRICT_SAME_ENGINE = True

    def __init__(self, nc, ctx):
        self.nc = nc
        self.ctx = ctx
        self.qsem = {e: ctx.enter_context(nc.semaphore(f"q_{e}")) for e in self.ENGS}
        self.qc = {e: 0 for e in self.ENGS}
        self.dsem = {}
        self.queue = {e: [] for e in self.ENGS}
        self.wr = {}
        self.rd = {}
        self.clock = {e: {} for e in self.ENGS}

    def _semh(self, kk):
        return self.qsem[kk[1]] if kk[0] == "q" else self.dsem[kk[1]][0]

    def _needs(self, eng, dma, reads, writes, extra):
        deps = []
        for k in reads:
            for ev in self.wr.get(k, ()):
                deps.append((ev, False))
        for k in writes:
            for ev in self.wr.get(k, ()):
                deps.append((ev, True))
            for rev in self.rd.get(k, {}).values():
                deps.append((rev, True))
        for ev in extra:
            deps.append((ev, False))
        waits = {}
        myclock = self.clock[eng]
        for (kk, cnt, evclock), order_only in deps:
            if (order_only and kk == ("q", eng) and not dma
                    and not self.STRICT_SAME_ENGINE):
                continue
            if myclock.get(kk, 0) >= cnt:
                continue
            waits[kk] = max(waits.get(kk, 0), cnt)
            for ck, cv in evclock.items():
                if myclock.get(ck, 0) < cv:
                    myclock[ck] = cv
            myclock[kk] = max(myclock.get(kk, 0), cnt)
        return waits

    def _record(self, eng, kk, cnt, reads, writes):
        evclock = dict(self.clock[eng])
        evclock[kk] = cnt
        ev = (kk, cnt, evclock)
        for k in writes:
            evs = [e for e in self.wr.get(k, ()) if e[0] != kk]
            evs.append(ev)
            self.wr[k] = evs
            self.rd[k] = {}
        for k in reads:
            self.rd.setdefault(k, {})[kk] = ev
        return ev

    def op(self, eng, fn, reads=(), writes=(), extra=()):
        waits = self._needs(eng, False, reads, writes, extra)
        self.qc[eng] += 1
        ev = self._record(eng, ("q", eng), self.qc[eng], reads, writes)
        self.queue[eng].append((waits, fn, ("q", eng), 1))
        return ev

    def dma(self, eng, fn, reads=(), writes=(), extra=(), n=1, sem=None):
        assert sem is not None, "dma ops need an explicit sem stream name"
        waits = self._needs(eng, True, reads, writes, extra)
        if sem not in self.dsem:
            self.dsem[sem] = [self.ctx.enter_context(self.nc.semaphore(f"dma_{sem}")), 0]
        self.dsem[sem][1] += 16 * n
        ev = self._record(eng, ("dma", sem), self.dsem[sem][1], reads, writes)
        self.queue[eng].append((waits, fn, ("dma", sem), n))
        return ev

    def barrier_all(self, eng="sync"):
        waits = {}
        for e in self.ENGS:
            if self.qc[e] and e != eng:
                waits[("q", e)] = self.qc[e]
        for name, (h, cnt) in self.dsem.items():
            waits[("dma", name)] = cnt
        self.queue[eng].append((waits, None, ("q", eng), 0))

    def finalize(self):
        nc = self.nc
        with nc.Block() as block:
            for ename in self.ENGS:
                if not self.queue[ename]:
                    continue
                def make_body(ename):
                    def body(eng):
                        for waits, fn, kk, n in self.queue[ename]:
                            for wkk, v in sorted(waits.items()):
                                eng.wait_ge(self._semh(wkk), v)
                            if fn is None:
                                continue
                            insts = fn(eng)
                            if kk[0] == "dma":
                                if not isinstance(insts, (list, tuple)):
                                    insts = [insts]
                                assert len(insts) == n, f"{ename}: dma op declared n={n} got {len(insts)}"
                                for inst in insts:
                                    inst.then_inc(self.dsem[kk[1]][0], 16)
                            else:
                                if isinstance(insts, (list, tuple)):
                                    insts = insts[-1]
                                insts.then_inc(self.qsem[ename], 1)
                    return body
                getattr(block, ename)(make_body(ename))


def tile_w_out_in(w):
    O, K = w.shape
    return np.ascontiguousarray(
        w.reshape(O // 128, 128, K // 128, 128).transpose(0, 2, 3, 1))


def tile_w_in_out(w):
    K, O = w.shape
    return np.ascontiguousarray(
        w.reshape(K // 128, 128, O // 128, 128).transpose(2, 0, 1, 3))


def tile_w_out_in_wide(w, ow=512):
    O, K = w.shape
    return np.ascontiguousarray(
        w.reshape(O // ow, ow, K // 128, 128).transpose(0, 2, 3, 1))


def emit_ln_tm(p, xs, xkeys, out_ap, out_key, stats, i, sq, eps_col, n=D,
               sqk="ln.sq", pref="ln"):
    m0 = stats["m0"]; s0 = stats["s0"]; m = stats["m"]; var = stats["var"]; rstd = stats["rstd"]
    p.op("vector", lambda v: v.reduce_sum(m0[:, i:i + 1], xs, axis=AX.X),
         reads=xkeys, writes=[f"{pref}.m0.{i}"])
    p.op("scalar", lambda s: s.activation(sq[:, :], xs, AF.Square),
         reads=xkeys, writes=[sqk])
    p.op("vector", lambda v: v.reduce_sum(s0[:, i:i + 1], sq[:, :], axis=AX.X),
         reads=[sqk], writes=[f"{pref}.s0.{i}"])
    p.op("vector", lambda v: v.tensor_scalar_mul(m[:, i:i + 1], m0[:, i:i + 1], 1.0 / n),
         reads=[f"{pref}.m0.{i}"], writes=[f"{pref}.m.{i}"])
    p.op("vector", lambda v: v.tensor_scalar(
            var[:, i:i + 1], s0[:, i:i + 1], 1.0 / n, None, ALU.mult),
         reads=[f"{pref}.s0.{i}"], writes=[f"{pref}.va.{i}"])
    p.op("vector", lambda v: v.scalar_tensor_tensor(
            var[:, i:i + 1], m[:, i:i + 1], m[:, i:i + 1], var[:, i:i + 1],
            ALU.mult, ALU.subtract),
         reads=[f"{pref}.m.{i}", f"{pref}.va.{i}"], writes=[f"{pref}.var.{i}"])
    p.op("scalar", lambda s: s.activation(rstd[:, i:i + 1], var[:, i:i + 1], AF.Sqrt,
                                          bias=eps_col, scale=-1.0),
         reads=[f"{pref}.var.{i}", "consts"], writes=[f"{pref}.std.{i}"])
    p.op("vector", lambda v: v.reciprocal(rstd[:, i:i + 1], rstd[:, i:i + 1]),
         reads=[f"{pref}.std.{i}"], writes=[f"{pref}.rstd.{i}"])
    p.op("vector", lambda v: v.tensor_scalar(
            out_ap, xs, m[:, i:i + 1], rstd[:, i:i + 1], ALU.subtract, ALU.mult),
         reads=list(xkeys) + [f"{pref}.m.{i}", f"{pref}.rstd.{i}"], writes=[out_key])


def emit_transpose_tm_to_fm(p, src_tm, src_keys, dst_fm, dst_key, ident, pbanks, pbank_keys,
                            n_ttiles, n_dchunks, t_per=1024):
    grp = 8
    nb = len(pbanks)
    g = 0
    for i in range(n_ttiles):
        for kc0 in range(0, n_dchunks, grp):
            bank = pbanks[g % nb]
            bkey = pbank_keys[g % nb]
            kcs = list(range(kc0, min(kc0 + grp, n_dchunks)))
            def f_tr(t, i=i, kcs=kcs, bank=bank):
                insts = []
                for j, kc in enumerate(kcs):
                    insts.append(t.transpose(
                        bank[:, j * 128:(j + 1) * 128],
                        src_tm[:, i * (128 * n_dchunks) + kc * 128: i * (128 * n_dchunks) + (kc + 1) * 128],
                        ident[:, :]))
                return insts
            p.op("tensor", f_tr, reads=[src_keys(i)], writes=[bkey], extra=())
            def f_ev(v, i=i, kcs=kcs, bank=bank):
                dst3 = dst_fm[:, :].rearrange("p (kc t) -> p kc t", t=t_per)
                dst = dst3[:, kcs[0]:kcs[0] + len(kcs), i * 128:(i + 1) * 128]
                return v.tensor_copy(dst, bank[:, 0:len(kcs) * 128].rearrange("p (kc t) -> p kc t", t=128))
            p.op("vector", f_ev, reads=[bkey], writes=[f"{dst_key}.ev.{i}.{kc0}"])
            g += 1


def build_launch_A():
    nc = bass.Bass("TRN2", target_bir_lowering=False)
    xh = nc.dram_tensor("xh", [T, D], F32, kind="ExternalInput").ap()
    w_in = nc.dram_tensor("w_in", [64, DC, 128, 128], F32, kind="ExternalInput").ap()
    ident = nc.dram_tensor("ident", [128, 128], F32, kind="ExternalInput").ap()
    z_fm = nc.dram_tensor("z_fm", [D, T], BF16, kind="ExternalOutput").ap()
    xbc_fm = nc.dram_tensor("xbc_fm", [3 * D, T], BF16, kind="ExternalOutput").ap()
    xn0b = nc.dram_tensor("xn0b", [D, T], BF16, kind="ExternalOutput").ap()

    ctx = ExitStack()
    xt = ctx.enter_context(nc.sbuf_tensor("xt", [128, TTI * D], F32))
    xnb = ctx.enter_context(nc.sbuf_tensor("xnb", [128, TTI * D], BF16))
    xn_fm = ctx.enter_context(nc.sbuf_tensor("xn_fm", [128, DC * T], BF16))
    identb = ctx.enter_context(nc.sbuf_tensor("identb", [128, 128], BF16))
    sq = ctx.enter_context(nc.sbuf_tensor("sq", [128, D], F32))
    stats = {k: ctx.enter_context(nc.sbuf_tensor(f"st_{k}", [128, TTI], F32))
             for k in ("m0", "s0", "m", "var", "rstd")}
    wsl = [ctx.enter_context(nc.sbuf_tensor(f"wsl{j}", [128, DC * 128], BF16)) for j in range(2)]
    stage = [ctx.enter_context(nc.sbuf_tensor(f"stg{j}", [128, T], BF16)) for j in range(2)]
    ptr = [ctx.enter_context(nc.psum_tensor(f"ptr{j}", [128, 512], F32)) for j in range(2)]
    pmm = [ctx.enter_context(nc.psum_tensor(f"pmm{j}", [128, 512], F32)) for j in range(4)]

    consts = ctx.enter_context(nc.sbuf_tensor("consts", [128, 4], F32))

    p = Prog(nc, ctx)
    p.op("vector", lambda v: v.memset(consts[:, 0:1], 1e-5), writes=["consts"])

    p.dma("gpsimd", lambda g: g.dma_start(identb[:, :], ident[:, :]), writes=["identb"], sem="identb")
    p.dma("gpsimd", lambda g: g.dma_start(
        xt[:, :].rearrange("p (tt d) -> p tt d", d=D),
        xh.rearrange("(tt p) d -> p tt d", p=128)), writes=[f"xt.{i}" for i in range(TTI)], sem="xt")

    for i in range(TTI):
        emit_ln_tm(p, xt[:, i * D:(i + 1) * D], [f"xt.{i}"],
                   xnb[:, i * D:(i + 1) * D], f"ln.out.{i}", stats, i, sq,
                   eps_col=consts[:, 0:1])

    ptr_b = [b.bitcast(BF16) for b in ptr]
    emit_transpose_tm_to_fm(p, xnb, lambda i: f"ln.out.{i}", xn_fm, "xn_fm",
                            identb, ptr_b, ["ptrb0", "ptrb1"], TTI, DC, t_per=T)
    xn_fm_ready = [f"xn_fm.ev.{i}.{kc0}" for i in range(TTI) for kc0 in (0, 8)]

    p.dma("sync", lambda s: s.dma_start(
        xn0b.rearrange("(kc p) t -> p kc t", p=128),
        xn_fm[:, :].rearrange("p (kc t) -> p kc t", t=T)),
        reads=xn_fm_ready, writes=["xn0b_out"], sem="xn0b_out")

    for oc in range(64):
        par = oc % 2
        p.dma("gpsimd", lambda g, oc=oc, par=par: g.dma_start(
            wsl[par][:, :].rearrange("p (kc o) -> p kc o", o=128),
            w_in[oc].rearrange("kc k o -> k kc o")),
            reads=[], writes=[f"wsl.{par}"], sem=f"wsl{par}")
        pbs = [pmm[par * 2], pmm[par * 2 + 1]]
        pbks = [f"pmm.{par * 2}", f"pmm.{par * 2 + 1}"]
        def f_mm(t, oc=oc, par=par, pbs=pbs):
            for kc in range(DC):
                for nh in range(2):
                    inst = t.matmul(
                        pbs[nh][:, :],
                        wsl[par][:, kc * 128:(kc + 1) * 128],
                        xn_fm[:, kc * T + nh * 512: kc * T + (nh + 1) * 512],
                        start=(kc == 0), stop=(kc == DC - 1))
            return inst
        p.op("tensor", f_mm, reads=[f"wsl.{par}"] + xn_fm_ready, writes=pbks)
        for nh in range(2):
            p.op("vector", lambda v, par=par, nh=nh, pb=pbs[nh]: v.tensor_copy(
                stage[par][:, nh * 512:(nh + 1) * 512], pb[:, :]),
                reads=[pbks[nh]], writes=[f"stage.{par}.{nh}"])
        dst = z_fm if oc < 16 else xbc_fm
        row = oc * 128 if oc < 16 else (oc - 16) * 128
        p.dma("sync", lambda s, dst=dst, row=row, par=par: s.dma_start(
            dst[row:row + 128, :], stage[par][:, :]),
            reads=[f"stage.{par}.0", f"stage.{par}.1"], writes=[f"out.{oc}"], sem=f"st{par}")

    p.barrier_all("sync")
    p.finalize()
    ctx.close()
    return nc


def _run(nc, in_maps):
    return run_bass_kernel_spmd(nc, in_maps, core_ids=list(range(NCORES))).results


def build_launch_A2():
    nc = bass.Bass("TRN2", target_bir_lowering=False)
    xn0b = nc.dram_tensor("xn0b", [D, T], BF16, kind="ExternalInput").ap()
    w_gu = nc.dram_tensor("w_gu", [128, DC, 128, 128], F32, kind="ExternalInput").ap()
    w_d = nc.dram_tensor("w_d", [4, 64, 128, 512], F32, kind="ExternalInput").ap()
    g0 = nc.dram_tensor("g0", [T, D], BF16, kind="ExternalOutput").ap()

    ctx = ExitStack()
    xn_fm = ctx.enter_context(nc.sbuf_tensor("xn_fm", [128, DC * T], BF16))
    h = ctx.enter_context(nc.sbuf_tensor("h", [128, 64 * T], BF16))
    wsl = [ctx.enter_context(nc.sbuf_tensor(f"wsl{j}", [128, DC * 128], BF16)) for j in range(4)]
    wdsl = [ctx.enter_context(nc.sbuf_tensor(f"wdsl{j}", [128, 8 * 512], BF16)) for j in range(2)]
    sg = ctx.enter_context(nc.sbuf_tensor("sg", [128, T], F32))
    stage = [ctx.enter_context(nc.sbuf_tensor(f"stg{j}", [128, 512], BF16)) for j in range(8)]
    pmm = [ctx.enter_context(nc.psum_tensor(f"pmm{j}", [128, 512], F32)) for j in range(8)]

    p = Prog(nc, ctx)
    p.dma("gpsimd", lambda g: g.dma_start(
        xn_fm[:, :].rearrange("p (kc t) -> p kc t", t=T),
        xn0b.rearrange("(kc p) t -> p kc t", p=128)), writes=["xn_fm"], sem="xn_fm")

    for i in range(64):
        par = i % 2
        wg, wu = wsl[par * 2], wsl[par * 2 + 1]
        p.dma("gpsimd", lambda g, i=i, wg=wg: g.dma_start(
            wg[:, :].rearrange("p (kc o) -> p kc o", o=128),
            w_gu[i].rearrange("kc k o -> k kc o")), writes=[f"wg.{par}"], sem=f"wg{par}")
        p.dma("gpsimd", lambda g, i=i, wu=wu: g.dma_start(
            wu[:, :].rearrange("p (kc o) -> p kc o", o=128),
            w_gu[64 + i].rearrange("kc k o -> k kc o")), writes=[f"wu.{par}"], sem=f"wu{par}")
        pg = [pmm[par * 4], pmm[par * 4 + 1]]
        pu = [pmm[par * 4 + 2], pmm[par * 4 + 3]]
        pgk = [f"pmm.{par * 4}", f"pmm.{par * 4 + 1}"]
        puk = [f"pmm.{par * 4 + 2}", f"pmm.{par * 4 + 3}"]
        def f_mm(t, wg=wg, wu=wu, pg=pg, pu=pu):
            for kc in range(DC):
                for nh in range(2):
                    rhs = xn_fm[:, kc * T + nh * 512: kc * T + (nh + 1) * 512]
                    t.matmul(pg[nh][:, :], wg[:, kc * 128:(kc + 1) * 128], rhs,
                             start=(kc == 0), stop=(kc == DC - 1))
                    inst = t.matmul(pu[nh][:, :], wu[:, kc * 128:(kc + 1) * 128], rhs,
                                    start=(kc == 0), stop=(kc == DC - 1))
            return inst
        p.op("tensor", f_mm, reads=[f"wg.{par}", f"wu.{par}", "xn_fm"], writes=pgk + puk)
        for nh in range(2):
            p.op("scalar", lambda s, nh=nh, pg=pg: s.activation(
                sg[:, nh * 512:(nh + 1) * 512], pg[nh][:, :], AF.Silu),
                reads=[pgk[nh]], writes=[f"sg.{nh}"])
            p.op("vector", lambda v, i=i, nh=nh, pu=pu: v.tensor_mul(
                h[:, i * T + nh * 512: i * T + (nh + 1) * 512],
                sg[:, nh * 512:(nh + 1) * 512], pu[nh][:, :]),
                reads=[f"sg.{nh}", puk[nh]], writes=[f"h.{i}"])

    for oc in range(4):
        for icg in range(8):
            par = icg % 2
            p.dma("gpsimd", lambda g, oc=oc, icg=icg, par=par: g.dma_start(
                wdsl[par][:, :].rearrange("p (ic o) -> p ic o", o=512),
                w_d[oc, icg * 8:(icg + 1) * 8].rearrange("ic k o -> k ic o")),
                writes=[f"wd.{par}"], sem=f"wd{par}")
            def f_dn(t, oc=oc, icg=icg, par=par):
                for ici in range(8):
                    ic = icg * 8 + ici
                    for tt in range(TTI):
                        inst = t.matmul(
                            pmm[tt][:, :],
                            h[:, ic * T + tt * 128:(ic * T) + (tt + 1) * 128],
                            wdsl[par][:, ici * 512:(ici + 1) * 512],
                            start=(ic == 0), stop=(ic == 63))
                return inst
            p.op("tensor", f_dn,
                 reads=[f"wd.{par}"] + [f"h.{icg * 8 + j}" for j in range(8)],
                 writes=[f"pmm.{tt}" for tt in range(TTI)])
        for tt in range(TTI):
            p.op("scalar", lambda s, tt=tt: s.activation(
                stage[tt][:, :], pmm[tt][:, :], AF.Silu),
                reads=[f"pmm.{tt}"], writes=[f"stage.{tt}"])
            p.dma("sync", lambda s, oc=oc, tt=tt: s.dma_start(
                g0[tt * 128:(tt + 1) * 128, oc * 512:(oc + 1) * 512], stage[tt][:, :]),
                reads=[f"stage.{tt}"], writes=[f"g0out.{oc}.{tt}"], sem=f"g0st{tt}")

    p.barrier_all("sync")
    p.finalize()
    ctx.close()
    return nc


def prep_host(inputs):
    hp = {}
    hp["ident"] = np.eye(128, dtype=np.float32)
    hp["w_in"] = tile_w_out_in(np.ascontiguousarray(inputs["L0_attn_W_in"][:8192]))
    hp["w_gu"] = tile_w_out_in(inputs["L0_ffn_w_gate_up"])
    hp["w_d"] = tile_w_out_in_wide(inputs["L0_ffn_w_down"], ow=512)
    return hp


def run_phase1(inputs, upto="Z"):
    x = np.asarray(inputs["L0_attn_u"])
    hp = prep_host(inputs)
    inter = {}

    def core_tok(core):
        b, h = core // 2, core % 2
        return b, slice(h * 1024, (h + 1) * 1024)

    nc = build_launch_A()
    im = []
    for core in range(NCORES):
        b, sl = core_tok(core)
        im.append({"xh": np.ascontiguousarray(x[b, sl]), "w_in": hp["w_in"],
                   "ident": hp["ident"]})
    ra = _run(nc, im)
    inter["A"] = ra
    if upto == "A":
        return inter

    nc = build_launch_A2()
    im = []
    for core in range(NCORES):
        im.append({"xn0b": np.ascontiguousarray(ra[core]["xn0b"]),
                   "w_gu": hp["w_gu"], "w_d": hp["w_d"]})
    ra2 = _run(nc, im)
    inter["A2"] = ra2
    if upto == "A2":
        return inter

    A_log = np.asarray(inputs["L0_attn_A_log"], np.float32)
    conv_w = np.asarray(inputs["L0_attn_conv_w"], np.float32)
    conv_b = np.asarray(inputs["L0_attn_conv_b"], np.float32)
    A_log_rep = np.repeat(A_log, 64)
    mask_ssd = np.vstack([np.triu(np.ones((64, 64), np.float32))] * 2)
    xbc_b = [np.concatenate([ra[2 * b]["xbc_fm"], ra[2 * b + 1]["xbc_fm"]], axis=1)
             for b in range(BATCH)]
    nc = build_launch_B()
    im = []
    for core in range(NCORES):
        b, hh = core // 2, core % 2
        ch = np.concatenate([np.arange(hh * 1024, hh * 1024 + 1024),
                             2048 + np.arange(hh * 1024, hh * 1024 + 1024),
                             4096 + np.arange(hh * 1024, hh * 1024 + 1024)])
        cw_dev = np.ascontiguousarray(
            conv_w[ch, 0, :].reshape(24, 128, 3).transpose(1, 0, 2).reshape(128, 72))
        cb_dev = np.ascontiguousarray(conv_b[ch].reshape(24, 128).T)
        alog_dev = np.ascontiguousarray(
            A_log_rep[hh * 1024:(hh + 1) * 1024].reshape(8, 128).T)
        im.append({
            "xp": np.ascontiguousarray(xbc_b[b][hh * 1024:(hh + 1) * 1024]),
            "Bp": np.ascontiguousarray(xbc_b[b][2048 + hh * 1024: 2048 + (hh + 1) * 1024]),
            "Cp": np.ascontiguousarray(xbc_b[b][4096 + hh * 1024: 4096 + (hh + 1) * 1024]),
            "cw": cw_dev, "cb": cb_dev, "alog": alog_dev,
            "ident": hp["ident"], "mask": mask_ssd,
        })
    rb = _run(nc, im)
    inter["B"] = rb
    if upto == "B":
        return inter

    w_out_t = tile_w_out_in_wide(np.asarray(inputs["L0_attn_W_out"], np.float32), 512)
    rmsw_dev = np.ascontiguousarray(
        np.asarray(inputs["L0_attn_rms_w"], np.float32).reshape(DC, 128).T)
    y_full = [np.concatenate([rb[2 * b]["y_fm"], rb[2 * b + 1]["y_fm"]], axis=0)
              for b in range(BATCH)]
    nc = build_launch_C()
    im = []
    for core in range(NCORES):
        b, h = core // 2, core % 2
        sl = slice(h * 1024, (h + 1) * 1024)
        im.append({
            "y_sl": np.ascontiguousarray(y_full[b][:, sl]),
            "z_fm": np.ascontiguousarray(ra[core]["z_fm"]),
            "xh": np.ascontiguousarray(x[b, sl]),
            "g0": np.ascontiguousarray(ra2[core]["g0"]),
            "w_out": w_out_t, "rmsw": rmsw_dev, "ident": hp["ident"],
        })
    rc = _run(nc, im)
    inter["C"] = rc
    if upto == "C":
        return inter

    lnw = np.asarray(inputs["L1_ffn_ln_weight"], np.float32)
    lnb = np.asarray(inputs["L1_ffn_ln_bias"], np.float32)
    lnwb_dev = np.ascontiguousarray(np.concatenate(
        [lnw.reshape(DC, 128).T, lnb.reshape(DC, 128).T], 1))
    gamrow = np.concatenate([np.asarray(inputs["L1_ffn_gamma"], np.float32),
                             np.asarray(inputs["L1_ffn_beta"], np.float32)])[None, :]
    w1t = tile_w_out_in_wide(np.asarray(inputs["L1_ffn_weight"], np.float32), 512)
    nc = build_launch_C2()
    im = [{"xn1b_fm": np.ascontiguousarray(rc[core]["xn1b_fm"]), "w1": w1t,
           "lnwb": lnwb_dev, "gamrow": gamrow, "ident": hp["ident"]}
          for core in range(NCORES)]
    rc2 = _run(nc, im)
    inter["C2"] = rc2
    if upto == "C2":
        return inter

    w_qkv = np.asarray(inputs["L1_attn_w_qkv"], np.float32)
    tiles = np.ascontiguousarray(w_qkv.reshape(DC, 128, 48, 128).transpose(2, 0, 1, 3))
    ln_s = np.asarray(inputs["L1_attn_ln_scale"], np.float32)
    ln_b = np.asarray(inputs["L1_attn_ln_bias"], np.float32)
    inv = 1.0 / 10000.0 ** (np.arange(0, 128, 2) / 128)
    tgrid = np.arange(SEQ)[:, None] * inv[None, :]
    cos2 = np.vstack([np.cos(tgrid).T] * 2).astype(np.float32)
    sin2 = np.vstack([np.sin(tgrid).T] * 2).astype(np.float32)
    cs2 = np.ascontiguousarray(np.hstack([cos2, sin2]))
    mask4 = np.zeros((128, 4 * 512), np.float32)
    for off in range(4):
        for pz in range(128):
            mask4[pz, off * 512 + off * 128 + pz:(off + 1) * 512] = 1.0
    xn1_full = [np.concatenate([rc[2 * b]["xn1b_fm"], rc[2 * b + 1]["xn1b_fm"]], axis=1)
                for b in range(BATCH)]
    nc = build_launch_D()
    im = []
    for core in range(NCORES):
        b, hh = core // 2, core % 2
        wsel = np.stack([np.stack([tiles[hh * 8 + hl], tiles[16 + hh * 8 + hl],
                                   tiles[32 + hh * 8 + hl]]) for hl in range(8)])
        im.append({"xn1_fm": np.ascontiguousarray(xn1_full[b]),
                   "wqkv": np.ascontiguousarray(wsel),
                   "lns": np.ascontiguousarray(np.stack([ln_s, ln_b], 1)),
                   "cs2": cs2, "mask4": mask4,
                   "swp": np.roll(np.eye(128, dtype=np.float32), 64, axis=1)})
    rd = _run(nc, im)
    inter["D"] = rd
    if upto == "D":
        return inter

    wout1t = tile_w_in_out_wide(np.asarray(inputs["L1_attn_w_out"], np.float32), 512)
    attn_full = [np.concatenate([rd[2 * b]["attn"], rd[2 * b + 1]["attn"]], axis=0)
                 for b in range(BATCH)]
    nc = build_launch_E()
    im = []
    for core in range(NCORES):
        b, h = core // 2, core % 2
        sl = slice(h * 1024, (h + 1) * 1024)
        im.append({"attn_sl": np.ascontiguousarray(attn_full[b][:, sl]),
                   "w_out1": wout1t,
                   "x1": np.ascontiguousarray(rc[core]["x1"]),
                   "g1": np.ascontiguousarray(rc2[core]["g1"])})
    re_ = _run(nc, im)
    inter["E"] = re_
    return inter


def kernel(**inputs):
    inter = run_phase1(inputs, upto="Z")
    re_ = inter["E"]
    out = np.zeros((BATCH, SEQ, D), dtype=np.float32)
    for core in range(NCORES):
        b, h = core // 2, core % 2
        out[b, h * 1024:(h + 1) * 1024] = re_[core]["x2"]
    return out


NT_B = 8
NCH = 32
CH = 64


def build_launch_B(debug=False, stage=3):
    nc = bass.Bass("TRN2", target_bir_lowering=False)
    xp = nc.dram_tensor("xp", [1024, SEQ], BF16, kind="ExternalInput").ap()
    Bp = nc.dram_tensor("Bp", [1024, SEQ], BF16, kind="ExternalInput").ap()
    Cp = nc.dram_tensor("Cp", [1024, SEQ], BF16, kind="ExternalInput").ap()
    cw = nc.dram_tensor("cw", [128, 24 * 3], F32, kind="ExternalInput").ap()
    cb = nc.dram_tensor("cb", [128, 24], F32, kind="ExternalInput").ap()
    alog = nc.dram_tensor("alog", [128, NT_B], F32, kind="ExternalInput").ap()
    ident = nc.dram_tensor("ident", [128, 128], F32, kind="ExternalInput").ap()
    mask = nc.dram_tensor("mask", [128, 64], F32, kind="ExternalInput").ap()
    y_fm = nc.dram_tensor("y_fm", [1024, SEQ], BF16, kind="ExternalOutput").ap()
    if debug:
        xs_dbg = nc.dram_tensor("xs_dbg", [1024, SEQ], BF16, kind="ExternalOutput").ap()
        bs_dbg = nc.dram_tensor("bs_dbg", [1024, SEQ], BF16, kind="ExternalOutput").ap()
        cs_dbg = nc.dram_tensor("cs_dbg", [128, NT_B * NCH], F32, kind="ExternalOutput").ap()
        prev_dbg = nc.dram_tensor("prev_dbg", [128, NT_B * (NCH + 1)], F32, kind="ExternalOutput").ap()
        xt_dbg = nc.dram_tensor("xt_dbg", [128, NT_B * SEQ], BF16, kind="ExternalOutput").ap()

    ctx = ExitStack()
    xs = ctx.enter_context(nc.sbuf_tensor("xs", [128, NT_B * SEQ], BF16))
    Bs = ctx.enter_context(nc.sbuf_tensor("Bs", [128, NT_B * SEQ], BF16))
    Cs = ctx.enter_context(nc.sbuf_tensor("Cs", [128, NT_B * SEQ], BF16))
    xT = ctx.enter_context(nc.sbuf_tensor("xT", [128, NT_B * SEQ], BF16))
    cws = ctx.enter_context(nc.sbuf_tensor("cws", [128, 24 * 3], F32))
    cbs = ctx.enter_context(nc.sbuf_tensor("cbs", [128, 24], F32))
    alogs = ctx.enter_context(nc.sbuf_tensor("alogs", [128, NT_B], F32))
    decay = ctx.enter_context(nc.sbuf_tensor("decay", [128, NT_B], F32))
    identb = ctx.enter_context(nc.sbuf_tensor("identb", [128, 128], BF16))
    maskb = ctx.enter_context(nc.sbuf_tensor("maskb", [128, 64], F32))
    acc_a = ctx.enter_context(nc.sbuf_tensor("acc_a", [128, SEQ], F32))
    acc_b = ctx.enter_context(nc.sbuf_tensor("acc_b", [128, SEQ], F32))
    prod = [ctx.enter_context(nc.sbuf_tensor(f"prod{j}", [128, SEQ], BF16)) for j in range(2)]
    cs_all = ctx.enter_context(nc.sbuf_tensor("cs_all", [128, NT_B * NCH], F32))
    prevb = ctx.enter_context(nc.sbuf_tensor("prevb", [128, NT_B * (NCH + 1)], F32))
    onesb = ctx.enter_context(nc.sbuf_tensor("onesb", [128, 1], BF16))
    onesrow = ctx.enter_context(nc.sbuf_tensor("onesrow", [1, 128], BF16))
    gt_sb = [ctx.enter_context(nc.sbuf_tensor(f"gt{j}", [128, 512], BF16)) for j in range(2)]
    yoff_sb = [ctx.enter_context(nc.sbuf_tensor(f"yoff{j}", [1, 512], BF16)) for j in range(4)]
    y_st = [ctx.enter_context(nc.sbuf_tensor(f"yst{j}", [128, 512], BF16)) for j in range(2)]
    pb = [ctx.enter_context(nc.psum_tensor(f"pb{j}", [128, 512], F32)) for j in range(8)]

    p = Prog(nc, ctx)

    for name, dst, src in (("xs", xs, xp), ("Bs", Bs, Bp), ("Cs", Cs, Cp)):
        p.dma("gpsimd", lambda g, dst=dst, src=src: g.dma_start(
            dst[:, :].rearrange("p (t s) -> p t s", s=SEQ),
            src.rearrange("(t p) s -> p t s", p=128)),
            writes=[f"{name}.pre.{t}" for t in range(NT_B)], sem=f"in_{name}")
    p.dma("gpsimd", lambda g: g.dma_start(cws[:, :], cw[:, :]), writes=["cws"], sem="cws")
    p.dma("gpsimd", lambda g: g.dma_start(cbs[:, :], cb[:, :]), writes=["cbs"], sem="cbs")
    p.dma("gpsimd", lambda g: g.dma_start(alogs[:, :], alog[:, :]), writes=["alogs"], sem="alogs")
    p.dma("gpsimd", lambda g: g.dma_start(identb[:, :], ident[:, :]), writes=["identb"], sem="identb")
    p.dma("gpsimd", lambda g: g.dma_start(maskb[:, :], mask[:, :]), writes=["maskb"], sem="maskb")
    p.op("vector", lambda v: v.memset(onesb[:, :], 1.0), writes=["onesb"])
    p.op("vector", lambda v: v.memset(onesrow[:, :], 1.0), writes=["onesrow"])
    p.op("vector", lambda v: v.memset(pb[0][:, :], 0.0), writes=["pb.0"])
    p.op("vector", lambda v: v.memset(pb[1][:, :], 0.0), writes=["pb.1"])
    p.op("vector", lambda v: v.memset(prevb[:, :], 0.0), writes=["prevb.z"])
    p.op("scalar", lambda s: s.activation(decay[:, :], alogs[:, :], AF.Exp, scale=63.0),
         reads=["alogs"], writes=["decay"])

    def conv_tile(part, buf, key, t, gidx):
        src = buf[:, t * SEQ:(t + 1) * SEQ]
        w0 = cws[:, gidx * 3 + 0: gidx * 3 + 1]
        w1 = cws[:, gidx * 3 + 1: gidx * 3 + 2]
        w2 = cws[:, gidx * 3 + 2: gidx * 3 + 3]
        bcol = cbs[:, gidx:gidx + 1]
        kpre = f"{key}.pre.{t}"
        p.op("vector", lambda v: v.memset(acc_a[:, 0:1], 0.0), writes=["acc_a"])
        p.op("vector", lambda v: v.tensor_scalar_mul(acc_a[:, 1:SEQ], src[:, 0:SEQ - 1], w0),
             reads=[kpre, "cws"], writes=["acc_a"])
        p.op("vector", lambda v: v.scalar_tensor_tensor(
                acc_b[:, :], src, w1, acc_a[:, :], ALU.mult, ALU.add),
             reads=[kpre, "cws", "acc_a"], writes=["acc_b"])
        p.op("vector", lambda v: v.tensor_copy(acc_a[:, SEQ - 1:SEQ], acc_b[:, SEQ - 1:SEQ]),
             reads=["acc_b"], writes=["acc_a"])
        p.op("vector", lambda v: v.scalar_tensor_tensor(
                acc_a[:, 0:SEQ - 1], src[:, 1:SEQ], w2, acc_b[:, 0:SEQ - 1],
                ALU.mult, ALU.add),
             reads=[kpre, "cws", "acc_b"], writes=["acc_a"])
        p.op("scalar", lambda s: s.activation(src, acc_a[:, :], AF.Silu, bias=bcol),
             reads=["acc_a", "cbs", kpre], writes=[f"{key}.{t}"])

    for t in range(NT_B):
        conv_tile("x", xs, "xs", t, t)
    for t in range(NT_B):
        conv_tile("B", Bs, "Bs", t, 8 + t)
    for t in range(NT_B):
        conv_tile("C", Cs, "Cs", t, 16 + t)

    for j in range(NT_B):
        pj = prod[j % 2]
        p.op("vector", lambda v, j=j, pj=pj: v.tensor_tensor(
                pj[:, :], xs[:, j * SEQ:(j + 1) * SEQ], Bs[:, j * SEQ:(j + 1) * SEQ], ALU.mult),
             reads=[f"xs.{j}", f"Bs.{j}"], writes=[f"prodbuf.{j % 2}"])
        p.op("vector", lambda v, j=j, pj=pj: v.reduce_sum(
                cs_all[:, j * NCH:(j + 1) * NCH].unsqueeze(2),
                pj[:, :].rearrange("p (c s) -> p c s", s=CH), axis=AX.X),
             reads=[f"prodbuf.{j % 2}"], writes=[f"cs.{j}"])
        p.op("vector", lambda v, j=j: v.tensor_tensor_scan(
                prevb[:, j * (NCH + 1) + 1: j * (NCH + 1) + 1 + NCH],
                decay[:, j:j + 1].to_broadcast([128, NCH]),
                cs_all[:, j * NCH:(j + 1) * NCH],
                0.0, ALU.mult, ALU.add),
             reads=[f"cs.{j}", "decay", "prevb.z"], writes=[f"prev.{j}"])

    grp = 8
    for j in range(NT_B):
        for sb0 in (0, 8):
            bank = pb[6 + (sb0 // 8) % 2].bitcast(BF16)
            bkey = f"pb.{6 + (sb0 // 8) % 2}"
            def f_tr(t, j=j, sb0=sb0, bank=bank):
                insts = []
                for i in range(grp):
                    sb = sb0 + i
                    insts.append(t.transpose(
                        bank[:, i * 128:(i + 1) * 128],
                        xs[:, j * SEQ + sb * 128: j * SEQ + (sb + 1) * 128],
                        identb[:, :]))
                return insts
            p.op("tensor", f_tr, reads=[f"xs.{j}", "identb"], writes=[bkey])
            def f_ev(v, j=j, sb0=sb0, bank=bank):
                dst = xT[:, :].rearrange("p (b ch) -> p b ch", ch=128)[:, j * 16 + sb0: j * 16 + sb0 + grp, :]
                return v.tensor_copy(dst, bank[:, :].rearrange("p (b ch) -> p b ch", ch=128))
            p.op("vector", f_ev, reads=[bkey], writes=[f"xT.{j}.{sb0}"])

    for j in range(NT_B if stage >= 3 else 0):
        pj = prod[j % 2]
        prev_j = prevb[:, j * (NCH + 1): j * (NCH + 1) + NCH]
        p.op("vector", lambda v, j=j, pj=pj, prev_j=prev_j: v.tensor_tensor(
                pj[:, :].rearrange("p (c s) -> p c s", s=CH),
                prev_j.unsqueeze(2).to_broadcast([128, NCH, CH]),
                Cs[:, j * SEQ:(j + 1) * SEQ].rearrange("p (c s) -> p c s", s=CH),
                ALU.mult),
             reads=[f"prev.{j}", f"Cs.{j}", "prevb.z"], writes=[f"prodbuf.{j % 2}"])
        for g in range(4):
            gpar = (j * 4 + g) % 2
            yb = pb[2 + gpar]
            ybk = f"pb.{2 + gpar}"
            ysb = y_st[gpar]
            for e in range(2):
                ob = pb[4 + e]
                obk = f"pb.{4 + e}"
                yo = yoff_sb[gpar * 2 + e]
                yok = f"yoff.{gpar * 2 + e}"
                p.op("tensor", lambda t, j=j, g=g, e=e, ob=ob: t.matmul(
                        ob[0:1, :], onesb[e * 64:e * 64 + 64, :],
                        prod[j % 2][e * 64:e * 64 + 64, g * 512:(g + 1) * 512],
                        start=True, stop=True),
                     reads=[f"prodbuf.{j % 2}", "onesb"], writes=[obk])
                p.op("vector", lambda v, yo=yo, ob=ob: v.tensor_copy(yo[:, :], ob[0:1, :]),
                     reads=[obk], writes=[yok])
                gb = pb[e]
                gbk = f"pb.{e}"
                def f_g(t, j=j, g=g, e=e, gb=gb):
                    for ci in range(8):
                        c = g * 8 + ci
                        rh = (c % 2) * 64
                        inst = t.matmul(
                            gb[rh:rh + 64, ci * 64:(ci + 1) * 64],
                            Bs[e * 64:e * 64 + 64, j * SEQ + c * CH: j * SEQ + (c + 1) * CH],
                            Cs[e * 64:e * 64 + 64, j * SEQ + c * CH: j * SEQ + (c + 1) * CH],
                            start=True, stop=True)
                    return inst
                p.op("tensor", f_g, reads=[f"Bs.{j}", f"Cs.{j}"], writes=[gbk])
                p.op("vector", lambda v, e=e, gb=gb: v.tensor_tensor(
                        gt_sb[e][:, :].rearrange("p (ci l) -> p ci l", l=64),
                        gb[:, :].rearrange("p (ci l) -> p ci l", l=64),
                        maskb[:, :].unsqueeze(1).to_broadcast([128, 8, 64]),
                        ALU.mult),
                     reads=[gbk, "maskb"], writes=[f"gts.{e}"])
            def f_y(t, j=j, g=g, gpar=gpar, yb=yb):
                for ci in range(8):
                    c = g * 8 + ci
                    sb = c // 2
                    mrow = (c % 2) * 64
                    for e in range(2):
                        t.matmul(
                            yb[e * 64:e * 64 + 64, ci * 64:(ci + 1) * 64],
                            xT[mrow:mrow + 64, (j * 16 + sb) * 128 + e * 64: (j * 16 + sb) * 128 + e * 64 + 64],
                            gt_sb[e][mrow:mrow + 64, ci * 64:(ci + 1) * 64],
                            start=True, stop=False)
                        inst = t.matmul(
                            yb[e * 64:e * 64 + 64, ci * 64:(ci + 1) * 64],
                            onesrow[0:1, 0:64],
                            yoff_sb[gpar * 2 + e][0:1, ci * 64:(ci + 1) * 64],
                            start=False, stop=True)
                return inst
            p.op("tensor", f_y,
                 reads=["gts.0", "gts.1", f"xT.{j}.0", f"xT.{j}.8", "onesrow",
                        f"yoff.{gpar * 2}", f"yoff.{gpar * 2 + 1}"],
                 writes=[ybk])
            p.op("vector", lambda v, yb=yb, ysb=ysb: v.tensor_copy(ysb[:, :], yb[:, :]),
                 reads=[ybk], writes=[f"yst.{gpar}"])
            p.dma("sync", lambda s, j=j, g=g, ysb=ysb: s.dma_start(
                    y_fm[j * 128:(j + 1) * 128, g * 512:(g + 1) * 512], ysb[:, :]),
                reads=[f"yst.{gpar}"], writes=[f"yout.{j}.{g}"], sem=f"yout{gpar}")

    if debug:
        p.dma("sync", lambda s: s.dma_start(
            xs_dbg.rearrange("(t p) s -> p t s", p=128),
            xs[:, :].rearrange("p (t s) -> p t s", s=SEQ)),
            reads=[f"xs.{t}" for t in range(NT_B)], writes=["dbg1"], sem="dbg")
        p.dma("sync", lambda s: s.dma_start(
            bs_dbg.rearrange("(t p) s -> p t s", p=128),
            Bs[:, :].rearrange("p (t s) -> p t s", s=SEQ)),
            reads=[f"Bs.{t}" for t in range(NT_B)], writes=["dbg1b"], sem="dbg")
        p.dma("sync", lambda s: s.dma_start(cs_dbg, cs_all[:, :]),
            reads=[f"cs.{t}" for t in range(NT_B)], writes=["dbg2"], sem="dbg")
        p.dma("sync", lambda s: s.dma_start(prev_dbg, prevb[:, :]),
            reads=[f"prev.{t}" for t in range(NT_B)], writes=["dbg3"], sem="dbg")
        p.dma("sync", lambda s: s.dma_start(xt_dbg, xT[:, :]),
            reads=[f"xT.{t}.{s0}" for t in range(NT_B) for s0 in (0, 8)], writes=["dbg4"], sem="dbg")

    p.barrier_all("sync")
    p.finalize()
    ctx.close()
    return nc


def build_launch_C():
    nc = bass.Bass("TRN2", target_bir_lowering=False)
    y_sl = nc.dram_tensor("y_sl", [D, T], BF16, kind="ExternalInput").ap()
    z_fm = nc.dram_tensor("z_fm", [D, T], BF16, kind="ExternalInput").ap()
    xh = nc.dram_tensor("xh", [T, D], F32, kind="ExternalInput").ap()
    g0 = nc.dram_tensor("g0", [T, D], BF16, kind="ExternalInput").ap()
    w_out = nc.dram_tensor("w_out", [4, DC, 128, 512], F32, kind="ExternalInput").ap()
    rmsw = nc.dram_tensor("rmsw", [128, DC], F32, kind="ExternalInput").ap()
    ident = nc.dram_tensor("ident", [128, 128], F32, kind="ExternalInput").ap()
    x1 = nc.dram_tensor("x1", [T, D], F32, kind="ExternalOutput").ap()
    xn1b_fm = nc.dram_tensor("xn1b_fm", [D, T], BF16, kind="ExternalOutput").ap()

    ctx = ExitStack()
    ybuf = ctx.enter_context(nc.sbuf_tensor("ybuf", [128, DC * T], BF16))
    zbuf = ctx.enter_context(nc.sbuf_tensor("zbuf", [128, DC * T], BF16))
    xt = ctx.enter_context(nc.sbuf_tensor("xt", [128, TTI * D], F32))
    g0sl = [ctx.enter_context(nc.sbuf_tensor(f"g0sl{j}", [128, TTI * 512], BF16)) for j in range(2)]
    wsl = [ctx.enter_context(nc.sbuf_tensor(f"wsl{j}", [128, 8 * 512], BF16)) for j in range(2)]
    zs = ctx.enter_context(nc.sbuf_tensor("zs", [128, T], F32))
    sqb = [ctx.enter_context(nc.sbuf_tensor(f"sqb{j}", [128, T], BF16)) for j in range(2)]
    ones128b = ctx.enter_context(nc.sbuf_tensor("ones128b", [128, 1], BF16))
    onesrow = ctx.enter_context(nc.sbuf_tensor("onesrow", [1, 128], F32))
    rrow = ctx.enter_context(nc.sbuf_tensor("rrow", [1, T], F32))
    rb = ctx.enter_context(nc.sbuf_tensor("rb", [128, T], F32))
    rmsws = ctx.enter_context(nc.sbuf_tensor("rmsws", [128, DC], F32))
    identb = ctx.enter_context(nc.sbuf_tensor("identb", [128, 128], BF16))
    consts = ctx.enter_context(nc.sbuf_tensor("consts", [128, 4], F32))
    tmp = ctx.enter_context(nc.sbuf_tensor("tmp", [128, 512], F32))
    xnb_s = ctx.enter_context(nc.sbuf_tensor("xnb_s", [128, D], BF16))
    xn1st = ctx.enter_context(nc.sbuf_tensor("xn1st", [128, D], BF16))
    sq = ctx.enter_context(nc.sbuf_tensor("sq", [128, D], F32))
    stats = {k: ctx.enter_context(nc.sbuf_tensor(f"st_{k}", [128, TTI], F32))
             for k in ("m0", "s0", "m", "var", "rstd")}
    pb = [ctx.enter_context(nc.psum_tensor(f"pb{j}", [128, 512], F32)) for j in range(8)]

    p = Prog(nc, ctx)
    p.op("vector", lambda v: v.memset(consts[:, 0:1], 1.1920929e-07), writes=["consts"])
    p.op("vector", lambda v: v.memset(consts[:, 1:2], 1e-5), writes=["consts"])
    p.op("vector", lambda v: v.memset(ones128b[:, :], 1.0), writes=["ones128b"])
    p.op("vector", lambda v: v.memset(onesrow[:, :], 1.0), writes=["onesrow"])
    p.dma("gpsimd", lambda g: g.dma_start(
        ybuf[:, :].rearrange("p (kc t) -> p kc t", t=T),
        y_sl.rearrange("(kc p) t -> p kc t", p=128)),
        writes=[f"y.{kc}" for kc in range(DC)], sem="y")
    p.dma("gpsimd", lambda g: g.dma_start(
        zbuf[:, :].rearrange("p (kc t) -> p kc t", t=T),
        z_fm.rearrange("(kc p) t -> p kc t", p=128)),
        writes=[f"z.{kc}" for kc in range(DC)], sem="z")
    p.dma("gpsimd", lambda g: g.dma_start(
        xt[:, :].rearrange("p (tt d) -> p tt d", d=D),
        xh.rearrange("(tt p) d -> p tt d", p=128)),
        writes=[f"xt.{tt}" for tt in range(TTI)], sem="xt")
    p.dma("gpsimd", lambda g: g.dma_start(rmsws[:, :], rmsw[:, :]), writes=["rmsws"], sem="rmsws")
    p.dma("gpsimd", lambda g: g.dma_start(identb[:, :], ident[:, :]), writes=["identb"], sem="identb")

    for kc in range(DC):
        par = kc % 2
        p.op("scalar", lambda s, kc=kc: s.activation(
                zs[:, :], zbuf[:, kc * T:(kc + 1) * T], AF.Silu),
             reads=[f"z.{kc}"], writes=["zs"])
        p.op("vector", lambda v, kc=kc: v.tensor_mul(
                zbuf[:, kc * T:(kc + 1) * T], ybuf[:, kc * T:(kc + 1) * T], zs[:, :]),
             reads=[f"y.{kc}", "zs"], writes=[f"z.{kc}"])
        p.op("scalar", lambda s, kc=kc, par=par: s.activation(
                sqb[par][:, :], zbuf[:, kc * T:(kc + 1) * T], AF.Square),
             reads=[f"z.{kc}"], writes=[f"sqb.{par}"])
        def f_st(t, kc=kc, par=par):
            insts = []
            for nh in range(2):
                insts.append(t.matmul(
                    pb[nh][0:1, :], ones128b[:, :], sqb[par][:, nh * 512:(nh + 1) * 512],
                    start=(kc == 0), stop=(kc == DC - 1)))
            return insts[-1]
        p.op("tensor", f_st, reads=[f"sqb.{par}", "ones128b"], writes=["pb.0", "pb.1"])
    for nh in range(2):
        p.op("vector", lambda v, nh=nh: v.tensor_copy(rrow[:, nh * 512:(nh + 1) * 512], pb[nh][0:1, :]),
             reads=[f"pb.{nh}"], writes=["rrow"])
    p.op("vector", lambda v: v.tensor_scalar_mul(rrow[:, :], rrow[:, :], 1.0 / D),
         reads=["rrow"], writes=["rrow"])
    p.op("scalar", lambda s: s.activation(rrow[:, :], rrow[:, :], AF.Sqrt, bias=consts[0:1, 0:1]),
         reads=["rrow", "consts"], writes=["rrow"])
    p.op("vector", lambda v: v.reciprocal(rrow[:, :], rrow[:, :]),
         reads=["rrow"], writes=["rrow"])
    def f_bc(t):
        for nh in range(2):
            inst = t.matmul(pb[2 + nh][:, :], onesrow[:, :], rrow[:, nh * 512:(nh + 1) * 512],
                            start=True, stop=True)
        return inst
    p.op("tensor", f_bc, reads=["rrow", "onesrow"], writes=["pb.2", "pb.3"])
    for nh in range(2):
        p.op("vector", lambda v, nh=nh: v.tensor_copy(rb[:, nh * 512:(nh + 1) * 512], pb[2 + nh][:, :]),
             reads=[f"pb.{2 + nh}"], writes=["rb"])
    for kc in range(DC):
        p.op("vector", lambda v, kc=kc: v.scalar_tensor_tensor(
                ybuf[:, kc * T:(kc + 1) * T], zbuf[:, kc * T:(kc + 1) * T],
                rmsws[:, kc:kc + 1], rb[:, :], ALU.mult, ALU.mult),
             reads=[f"z.{kc}", "rmsws", "rb"], writes=[f"yn.{kc}"])

    for oc in range(4):
        gpar = oc % 2
        p.dma("gpsimd", lambda g, oc=oc, gpar=gpar: g.dma_start(
            g0sl[gpar][:, :].rearrange("p (tt o) -> p tt o", o=512),
            g0.rearrange("(tt p) d -> p tt d", p=128)[:, :, oc * 512:(oc + 1) * 512]),
            writes=[f"g0sl.{gpar}"], sem=f"g0sl{gpar}")
        for kcg in range(2):
            par = kcg % 2
            p.dma("gpsimd", lambda g, oc=oc, kcg=kcg, par=par: g.dma_start(
                wsl[par][:, :].rearrange("p (kc o) -> p kc o", o=512),
                w_out[oc, kcg * 8:(kcg + 1) * 8].rearrange("kc k o -> k kc o")),
                writes=[f"wsl.{par}"], sem=f"wsl{par}")
            def f_mm(t, oc=oc, kcg=kcg, par=par):
                for kci in range(8):
                    kc = kcg * 8 + kci
                    for tt in range(TTI):
                        inst = t.matmul(
                            pb[tt][:, :],
                            ybuf[:, kc * T + tt * 128: kc * T + (tt + 1) * 128],
                            wsl[par][:, kci * 512:(kci + 1) * 512],
                            start=(kc == 0), stop=(kc == DC - 1))
                return inst
            p.op("tensor", f_mm,
                 reads=[f"wsl.{par}"] + [f"yn.{kcg * 8 + j}" for j in range(8)],
                 writes=[f"pb.{tt}" for tt in range(TTI)])
        for tt in range(TTI):
            p.op("vector", lambda v, oc=oc, tt=tt: v.tensor_mul(
                    tmp[:, :], pb[tt][:, :],
                    g0sl[oc % 2][:, tt * 512:(tt + 1) * 512]),
                 reads=[f"pb.{tt}", f"g0sl.{oc % 2}"], writes=["tmp"])
            p.op("vector", lambda v, oc=oc, tt=tt: v.tensor_add(
                    xt[:, tt * D + oc * 512: tt * D + (oc + 1) * 512],
                    xt[:, tt * D + oc * 512: tt * D + (oc + 1) * 512], tmp[:, :]),
                 reads=["tmp", f"xt.{tt}"], writes=[f"xt.{tt}"])

    for tt in range(TTI):
        p.dma("sync", lambda s, tt=tt: s.dma_start(
            x1[tt * 128:(tt + 1) * 128, :], xt[:, tt * D:(tt + 1) * D]),
            reads=[f"xt.{tt}"], writes=[f"x1out.{tt}"], sem="x1out")
        emit_ln_tm(p, xt[:, tt * D:(tt + 1) * D], [f"xt.{tt}"],
                   xnb_s[:, :], "xnb_s", stats, tt, sq,
                   eps_col=consts[:, 1:2], pref="ln1")
        for half in range(2):
            bank = pb[half].bitcast(BF16)
            bkey = f"pb.{half}"
            def f_tr(t, tt=tt, half=half, bank=bank):
                insts = []
                for j in range(8):
                    kc = half * 8 + j
                    insts.append(t.transpose(
                        bank[:, j * 128:(j + 1) * 128],
                        xnb_s[:, kc * 128:(kc + 1) * 128], identb[:, :]))
                return insts
            p.op("tensor", f_tr, reads=["xnb_s", "identb"], writes=[bkey])
            p.op("vector", lambda v, half=half, bank=bank: v.tensor_copy(
                    xn1st[:, half * 1024:(half + 1) * 1024], bank[:, 0:1024]),
                 reads=[bkey], writes=[f"xn1st.{half}"])
        p.dma("sync", lambda s, tt=tt: s.dma_start(
            xn1b_fm.rearrange("(kc p) t -> p kc t", p=128)[:, :, tt * 128:(tt + 1) * 128],
            xn1st[:, :].rearrange("p (kc t) -> p kc t", t=128)),
            reads=["xn1st.0", "xn1st.1"], writes=[f"xn1out.{tt}"], sem="xn1out")

    p.barrier_all("sync")
    p.finalize()
    ctx.close()
    return nc


def tile_w_in_out_wide(w, ow=512):
    K, O = w.shape
    return np.ascontiguousarray(
        w.reshape(K // 128, 128, O // ow, ow).transpose(2, 0, 1, 3))


def build_launch_C2():
    nc = bass.Bass("TRN2", target_bir_lowering=False)
    xn1b_fm = nc.dram_tensor("xn1b_fm", [D, T], BF16, kind="ExternalInput").ap()
    w1 = nc.dram_tensor("w1", [4, DC, 128, 512], F32, kind="ExternalInput").ap()
    lnwb = nc.dram_tensor("lnwb", [128, 2 * DC], F32, kind="ExternalInput").ap()
    gamrow = nc.dram_tensor("gamrow", [1, 2 * D], F32, kind="ExternalInput").ap()
    g1 = nc.dram_tensor("g1", [T, D], BF16, kind="ExternalOutput").ap()

    ctx = ExitStack()
    xfm = ctx.enter_context(nc.sbuf_tensor("xfm", [128, DC * T], BF16))
    lnx = ctx.enter_context(nc.sbuf_tensor("lnx", [128, DC * T], BF16))
    wsl = [ctx.enter_context(nc.sbuf_tensor(f"wsl{j}", [128, 8 * 512], BF16)) for j in range(2)]
    ssl = [ctx.enter_context(nc.sbuf_tensor(f"ssl{j}", [128, 8 * 512], BF16)) for j in range(2)]
    lnwbs = ctx.enter_context(nc.sbuf_tensor("lnwbs", [128, 2 * DC], F32))
    grows = ctx.enter_context(nc.sbuf_tensor("grows", [1, 2 * D], F32))
    grep = ctx.enter_context(nc.sbuf_tensor("grep", [128, D], F32))
    brep = ctx.enter_context(nc.sbuf_tensor("brep", [128, D], F32))
    sq = ctx.enter_context(nc.sbuf_tensor("sq", [128, T], BF16))
    ones128b = ctx.enter_context(nc.sbuf_tensor("ones128b", [128, 1], BF16))
    onesrow = ctx.enter_context(nc.sbuf_tensor("onesrow", [1, 128], F32))
    srow = ctx.enter_context(nc.sbuf_tensor("srow", [1, 2 * T], F32))
    r2row = ctx.enter_context(nc.sbuf_tensor("r2row", [1, T], F32))
    mrrow = ctx.enter_context(nc.sbuf_tensor("mrrow", [1, T], F32))
    r2b = ctx.enter_context(nc.sbuf_tensor("r2b", [128, T], F32))
    mrb = ctx.enter_context(nc.sbuf_tensor("mrb", [128, T], F32))
    amaxc = ctx.enter_context(nc.sbuf_tensor("amaxc", [128, 1], F32))
    amrow = ctx.enter_context(nc.sbuf_tensor("amrow", [1, 128], F32))
    scal1 = ctx.enter_context(nc.sbuf_tensor("scal1", [1, 1], F32))
    scalc = ctx.enter_context(nc.sbuf_tensor("scalc", [128, 1], F32))
    identf = ctx.enter_context(nc.sbuf_tensor("identf", [128, 128], F32))
    ident = nc.dram_tensor("ident", [128, 128], F32, kind="ExternalInput").ap()
    consts = ctx.enter_context(nc.sbuf_tensor("consts", [128, 4], F32))
    tmp = ctx.enter_context(nc.sbuf_tensor("tmp", [128, T], F32))
    tmp2 = ctx.enter_context(nc.sbuf_tensor("tmp2", [128, 512], F32))
    stage = ctx.enter_context(nc.sbuf_tensor("stage", [128, 512], BF16))
    pb = [ctx.enter_context(nc.psum_tensor(f"pb{j}", [128, 512], F32)) for j in range(8)]

    p = Prog(nc, ctx)
    p.op("vector", lambda v: v.memset(consts[:, 0:1], 1e-5), writes=["consts"])
    p.op("vector", lambda v: v.memset(ones128b[:, :], 1.0), writes=["ones128b"])
    p.op("vector", lambda v: v.memset(onesrow[:, :], 1.0), writes=["onesrow"])
    p.op("vector", lambda v: v.memset(amaxc[:, :], 0.0), writes=["amaxc"])
    p.dma("gpsimd", lambda g: g.dma_start(
        xfm[:, :].rearrange("p (kc t) -> p kc t", t=T),
        xn1b_fm.rearrange("(kc p) t -> p kc t", p=128)),
        writes=[f"x.{kc}" for kc in range(DC)], sem="x")
    p.dma("gpsimd", lambda g: g.dma_start(lnwbs[:, :], lnwb[:, :]), writes=["lnwbs"], sem="lnwbs")
    p.dma("gpsimd", lambda g: g.dma_start(grows[:, :], gamrow[:, :]), writes=["grows"], sem="grows")
    p.dma("gpsimd", lambda g: g.dma_start(identf[:, :], ident[:, :]), writes=["identf"], sem="identf")

    def f_grep(t):
        for i in range(4):
            t.matmul(pb[i % 4][:, :], onesrow[:, :], grows[0:1, i * 512:(i + 1) * 512],
                     start=True, stop=True)
            inst = t.matmul(pb[4 + i % 4][:, :], onesrow[:, :], grows[0:1, D + i * 512: D + (i + 1) * 512],
                            start=True, stop=True)
        return inst
    p.op("tensor", f_grep, reads=["grows", "onesrow"], writes=[f"pb.{j}" for j in range(8)])
    for i in range(4):
        p.op("vector", lambda v, i=i: v.tensor_copy(grep[:, i * 512:(i + 1) * 512], pb[i][:, :]),
             reads=[f"pb.{i}"], writes=["grep"])
        p.op("vector", lambda v, i=i: v.tensor_copy(brep[:, i * 512:(i + 1) * 512], pb[4 + i][:, :]),
             reads=[f"pb.{4 + i}"], writes=["brep"])

    for kc in range(DC):
        p.op("scalar", lambda s, kc=kc: s.activation(sq[:, :], xfm[:, kc * T:(kc + 1) * T], AF.Square),
             reads=[f"x.{kc}"], writes=["sq"])
        def f_st(t, kc=kc):
            for nh in range(2):
                t.matmul(pb[nh][0:1, :], ones128b[:, :],
                         xfm[:, kc * T + nh * 512: kc * T + (nh + 1) * 512],
                         start=(kc == 0), stop=(kc == DC - 1))
                inst = t.matmul(pb[2 + nh][0:1, :], ones128b[:, :], sq[:, nh * 512:(nh + 1) * 512],
                                start=(kc == 0), stop=(kc == DC - 1))
            return inst
        p.op("tensor", f_st, reads=[f"x.{kc}", "sq", "ones128b"],
             writes=["pb.0", "pb.1", "pb.2", "pb.3"])
    for nh in range(2):
        p.op("vector", lambda v, nh=nh: v.tensor_copy(srow[:, nh * 512:(nh + 1) * 512], pb[nh][0:1, :]),
             reads=[f"pb.{nh}"], writes=["srow"])
        p.op("vector", lambda v, nh=nh: v.tensor_copy(srow[:, T + nh * 512: T + (nh + 1) * 512], pb[2 + nh][0:1, :]),
             reads=[f"pb.{2 + nh}"], writes=["srow"])
    p.op("vector", lambda v: v.tensor_scalar_mul(srow[:, :], srow[:, :], 1.0 / D),
         reads=["srow"], writes=["srow"])
    p.op("vector", lambda v: v.tensor_mul(r2row[:, :], srow[:, 0:T], srow[:, 0:T]),
         reads=["srow"], writes=["r2row"])
    p.op("vector", lambda v: v.tensor_sub(r2row[:, :], r2row[:, :], srow[:, T:2 * T]),
         reads=["srow", "r2row"], writes=["r2row"])
    p.op("scalar", lambda s: s.activation(r2row[:, :], r2row[:, :], AF.Sqrt,
                                          bias=consts[0:1, 0:1], scale=-1.0),
         reads=["r2row", "consts"], writes=["r2row"])
    p.op("vector", lambda v: v.reciprocal(r2row[:, :], r2row[:, :]),
         reads=["r2row"], writes=["r2row"])
    p.op("vector", lambda v: v.tensor_mul(mrrow[:, :], srow[:, 0:T], r2row[:, :]),
         reads=["srow", "r2row"], writes=["mrrow"])
    def f_bc(t):
        for nh in range(2):
            t.matmul(pb[4 + nh][:, :], onesrow[:, :], r2row[0:1, nh * 512:(nh + 1) * 512],
                     start=True, stop=True)
            inst = t.matmul(pb[6 + nh][:, :], onesrow[:, :], mrrow[0:1, nh * 512:(nh + 1) * 512],
                            start=True, stop=True)
        return inst
    p.op("tensor", f_bc, reads=["r2row", "mrrow", "onesrow"],
         writes=["pb.4", "pb.5", "pb.6", "pb.7"])
    for nh in range(2):
        p.op("vector", lambda v, nh=nh: v.tensor_copy(r2b[:, nh * 512:(nh + 1) * 512], pb[4 + nh][:, :]),
             reads=[f"pb.{4 + nh}"], writes=["r2b"])
        p.op("vector", lambda v, nh=nh: v.tensor_copy(mrb[:, nh * 512:(nh + 1) * 512], pb[6 + nh][:, :]),
             reads=[f"pb.{6 + nh}"], writes=["mrb"])

    for kc in range(DC):
        p.op("vector", lambda v, kc=kc: v.tensor_mul(tmp[:, :], xfm[:, kc * T:(kc + 1) * T], r2b[:, :]),
             reads=[f"x.{kc}", "r2b"], writes=["tmpA"])
        p.op("vector", lambda v, kc=kc: v.tensor_sub(tmp[:, :], tmp[:, :], mrb[:, :]),
             reads=["tmpA", "mrb"], writes=["tmpA"])
        p.op("vector", lambda v, kc=kc: v.tensor_scalar(
                lnx[:, kc * T:(kc + 1) * T], tmp[:, :],
                lnwbs[:, kc:kc + 1], lnwbs[:, DC + kc: DC + kc + 1], ALU.mult, ALU.add),
             reads=["tmpA", "lnwbs"], writes=[f"lnx.{kc}"])

    raw = ctx.enter_context(nc.sbuf_tensor("raw", [128, TTI * D], BF16))
    amx_t = ctx.enter_context(nc.sbuf_tensor("amx_t", [128, 1], F32))
    for oc in range(4):
        for kcg in range(2):
            par = kcg % 2
            p.dma("gpsimd", lambda g, oc=oc, kcg=kcg, par=par: g.dma_start(
                wsl[par][:, :].rearrange("p (kc o) -> p kc o", o=512),
                w1[oc, kcg * 8:(kcg + 1) * 8].rearrange("kc k o -> k kc o")),
                writes=[f"wsl.{par}"], sem=f"wsl{par}")
            p.op("vector", lambda v, par=par: v.reduce_max(
                    amx_t[:, :], wsl[par][:, :], axis=AX.X, apply_absolute_value=True),
                 reads=[f"wsl.{par}"], writes=["amx_t"])
            p.op("vector", lambda v: v.tensor_max(amaxc[:, :], amaxc[:, :], amx_t[:, :]),
                 reads=["amx_t", "amaxc"], writes=["amaxc"])
            p.op("scalar", lambda s, par=par: s.activation(ssl[par][:, :], wsl[par][:, :], AF.Sign),
                 reads=[f"wsl.{par}"], writes=[f"ssl.{par}"])
            def f_mm(t, oc=oc, kcg=kcg, par=par):
                for kci in range(8):
                    kc = kcg * 8 + kci
                    for tt in range(TTI):
                        inst = t.matmul(
                            pb[tt][:, :],
                            lnx[:, kc * T + tt * 128: kc * T + (tt + 1) * 128],
                            ssl[par][:, kci * 512:(kci + 1) * 512],
                            start=(kc == 0), stop=(kc == DC - 1))
                return inst
            p.op("tensor", f_mm,
                 reads=[f"ssl.{par}"] + [f"lnx.{kcg * 8 + j}" for j in range(8)],
                 writes=[f"pb.{tt}" for tt in range(TTI)])
        for tt in range(TTI):
            p.op("vector", lambda v, oc=oc, tt=tt: v.tensor_copy(
                    raw[:, tt * D + oc * 512: tt * D + (oc + 1) * 512], pb[tt][:, :]),
                 reads=[f"pb.{tt}"], writes=[f"raw.{oc}.{tt}"])

    p.op("tensor", lambda t: t.transpose(pb[0][0:1, 0:128], amaxc[:, 0:1], identf[:, :]),
         reads=["amaxc", "identf"], writes=["pb.0"])
    p.op("vector", lambda v: v.tensor_copy(amrow[:, :], pb[0][0:1, 0:128]),
         reads=["pb.0"], writes=["amrow"])
    p.op("vector", lambda v: v.reduce_max(scal1[:, :], amrow[:, :], axis=AX.X),
         reads=["amrow"], writes=["scal1"])
    p.op("tensor", lambda t: t.matmul(pb[1][:, 0:1], onesrow[:, :], scal1[:, :], start=True, stop=True),
         reads=["scal1", "onesrow"], writes=["pb.1"])
    p.op("vector", lambda v: v.tensor_copy(scalc[:, :], pb[1][:, 0:1]),
         reads=["pb.1"], writes=["scalc"])
    p.op("vector", lambda v: v.tensor_scalar_mul(grep[:, :], grep[:, :], scalc[:, 0:1]),
         reads=["grep", "scalc"], writes=["grep"])

    for tt in range(TTI):
        for oc in range(4):
            p.op("vector", lambda v, tt=tt, oc=oc: v.tensor_mul(
                    tmp2[:, :], raw[:, tt * D + oc * 512: tt * D + (oc + 1) * 512],
                    grep[:, oc * 512:(oc + 1) * 512]),
                 reads=[f"raw.{oc}.{tt}", "grep"], writes=["tmp2"])
            p.op("vector", lambda v, oc=oc: v.tensor_add(
                    tmp2[:, :], tmp2[:, :], brep[:, oc * 512:(oc + 1) * 512]),
                 reads=["tmp2", "brep"], writes=["tmp2"])
            p.op("scalar", lambda s: s.activation(stage[:, :], tmp2[:, :], AF.Silu),
                 reads=["tmp2"], writes=["stage"])
            p.dma("sync", lambda s, tt=tt, oc=oc: s.dma_start(
                g1[tt * 128:(tt + 1) * 128, oc * 512:(oc + 1) * 512], stage[:, :]),
                reads=["stage"], writes=[f"g1out.{tt}.{oc}"], sem="g1out")

    p.barrier_all("sync")
    p.finalize()
    ctx.close()
    return nc


def build_launch_D():
    nc = bass.Bass("TRN2", target_bir_lowering=False)
    xn1_fm = nc.dram_tensor("xn1_fm", [D, SEQ], BF16, kind="ExternalInput").ap()
    wqkv = nc.dram_tensor("wqkv", [8, 3, DC, 128, 128], F32, kind="ExternalInput").ap()
    lns_i = nc.dram_tensor("lns", [128, 2], F32, kind="ExternalInput").ap()
    cs2 = nc.dram_tensor("cs2", [128, 2 * SEQ], F32, kind="ExternalInput").ap()
    mask4 = nc.dram_tensor("mask4", [128, 4 * 512], F32, kind="ExternalInput").ap()
    swp = nc.dram_tensor("swp", [128, 128], F32, kind="ExternalInput").ap()
    attn = nc.dram_tensor("attn", [1024, SEQ], BF16, kind="ExternalOutput").ap()

    ctx = ExitStack()
    xfm = ctx.enter_context(nc.sbuf_tensor("xfm", [128, DC * SEQ], BF16))
    v_all = ctx.enter_context(nc.sbuf_tensor("v_all", [128, 16 * 8 * 128], BF16))
    qb = ctx.enter_context(nc.sbuf_tensor("qb", [128, SEQ], BF16))
    kb = ctx.enter_context(nc.sbuf_tensor("kb", [128, SEQ], BF16))
    wslb = [ctx.enter_context(nc.sbuf_tensor(f"wslb{j}", [128, 3 * DC * 128], BF16)) for j in range(2)]
    cs2s = ctx.enter_context(nc.sbuf_tensor("cs2s", [128, 2 * SEQ], F32))
    mask4s = ctx.enter_context(nc.sbuf_tensor("mask4s", [128, 4 * 512], F32))
    lnss = ctx.enter_context(nc.sbuf_tensor("lnss", [128, 2], F32))
    ra = ctx.enter_context(nc.sbuf_tensor("ra", [128, 512], F32))
    rb_ = ctx.enter_context(nc.sbuf_tensor("rb_", [128, 512], BF16))
    swps = ctx.enter_context(nc.sbuf_tensor("swps", [128, 128], BF16))
    cbf = ctx.enter_context(nc.sbuf_tensor("cbf", [128, 512], F32))
    cbb = ctx.enter_context(nc.sbuf_tensor("cbb", [128, 512], BF16))
    sqh = ctx.enter_context(nc.sbuf_tensor("sqh", [128, 512], BF16))
    srow2 = ctx.enter_context(nc.sbuf_tensor("srow2", [1, 1024], F32))
    rrow2 = ctx.enter_context(nc.sbuf_tensor("rrow2", [1, 512], F32))
    mrrow2 = ctx.enter_context(nc.sbuf_tensor("mrrow2", [1, 512], F32))
    r2b = ctx.enter_context(nc.sbuf_tensor("r2b", [128, 512], F32))
    mrb = ctx.enter_context(nc.sbuf_tensor("mrb", [128, 512], F32))
    sa = ctx.enter_context(nc.sbuf_tensor("sa", [128, 512], F32))
    a4b = ctx.enter_context(nc.sbuf_tensor("a4b", [128, 512], BF16))
    drow = ctx.enter_context(nc.sbuf_tensor("drow", [1, 512], F32))
    rbq = ctx.enter_context(nc.sbuf_tensor("rbq", [128, 512], F32))
    attnb = ctx.enter_context(nc.sbuf_tensor("attnb", [128, SEQ], BF16))
    ones128b = ctx.enter_context(nc.sbuf_tensor("ones128b", [128, 1], BF16))
    onesrow = ctx.enter_context(nc.sbuf_tensor("onesrow", [1, 128], F32))
    consts = ctx.enter_context(nc.sbuf_tensor("consts", [128, 4], F32))
    pb = [ctx.enter_context(nc.psum_tensor(f"pb{j}", [128, 512], F32)) for j in range(8)]

    p = Prog(nc, ctx)
    p.op("vector", lambda v: v.memset(consts[:, 0:1], 1e-5), writes=["consts"])
    p.op("vector", lambda v: v.memset(ones128b[:, :], 1.0), writes=["ones128b"])
    p.op("vector", lambda v: v.memset(onesrow[:, :], 1.0), writes=["onesrow"])
    p.dma("gpsimd", lambda g: g.dma_start(
        xfm[:, :].rearrange("p (kc t) -> p kc t", t=SEQ),
        xn1_fm.rearrange("(kc p) t -> p kc t", p=128)),
        writes=[f"x.{kc}" for kc in range(DC)], sem="x")
    p.dma("gpsimd", lambda g: g.dma_start(cs2s[:, :], cs2[:, :]), writes=["cs2s"], sem="cs2s")
    p.dma("gpsimd", lambda g: g.dma_start(mask4s[:, :], mask4[:, :]), writes=["mask4s"], sem="mask4s")
    p.dma("gpsimd", lambda g: g.dma_start(lnss[:, :], lns_i[:, :]), writes=["lnss"], sem="lnss")
    p.dma("gpsimd", lambda g: g.dma_start(swps[:, :], swp[:, :]), writes=["swps"], sem="swps")

    for hl in range(8):
        par = hl % 2
        p.dma("gpsimd", lambda g, hl=hl, par=par: g.dma_start(
            wslb[par][:, :].rearrange("p (s kc o) -> p s kc o", kc=DC, o=128).rearrange("p s kc o -> p s kc o"),
            wqkv[hl].rearrange("s kc k o -> k s kc o")),
            writes=[f"wq.{par}"], sem=f"wq{par}")
        for tth in range(2):
            def f_v(t, hl=hl, par=par, tth=tth):
                for kc in range(DC):
                    for tt in range(TTI):
                        inst = t.matmul(
                            pb[tt][:, 0:128],
                            xfm[:, kc * SEQ + (tth * 8 + tt) * 128: kc * SEQ + (tth * 8 + tt + 1) * 128],
                            wslb[par][:, (2 * DC + kc) * 128:(2 * DC + kc + 1) * 128],
                            start=(kc == 0), stop=(kc == DC - 1))
                return inst
            p.op("tensor", f_v, reads=[f"wq.{par}"] + [f"x.{kc}" for kc in range(DC)],
                 writes=[f"pb.{tt}" for tt in range(TTI)])
            for tt in range(TTI):
                p.op("vector", lambda v, hl=hl, tth=tth, tt=tt: v.tensor_copy(
                        v_all[:, ((tth * 8 + tt) * 8 + hl) * 128: ((tth * 8 + tt) * 8 + hl + 1) * 128],
                        pb[tt][:, 0:128]),
                     reads=[f"pb.{tt}"], writes=[f"v.{hl}.{tth * 8 + tt}"])
        for side, dst in ((0, qb), (1, kb)):
            dk = "qb" if side == 0 else "kb"
            for nq in range(4):
                def f_qk(t, hl=hl, par=par, side=side, nq=nq):
                    for kc in range(DC):
                        inst = t.matmul(
                            pb[4][:, :],
                            wslb[par][:, (side * DC + kc) * 128:(side * DC + kc + 1) * 128],
                            xfm[:, kc * SEQ + nq * 512: kc * SEQ + (nq + 1) * 512],
                            start=(kc == 0), stop=(kc == DC - 1))
                    return inst
                p.op("tensor", f_qk, reads=[f"wq.{par}"] + [f"x.{kc}" for kc in range(DC)],
                     writes=["pb.4"])
                p.op("vector", lambda v, nq=nq: v.tensor_mul(
                        ra[:, :], pb[4][:, :], cs2s[:, nq * 512:(nq + 1) * 512]),
                     reads=["pb.4", "cs2s"], writes=["ra"])
                p.op("vector", lambda v, nq=nq: v.tensor_mul(
                        rb_[:, :], pb[4][:, :], cs2s[:, SEQ + nq * 512: SEQ + (nq + 1) * 512]),
                     reads=["pb.4", "cs2s"], writes=["rb_"])
                p.op("tensor", lambda t: t.matmul(pb[7][:, :], swps[:, :], rb_[:, :],
                                                  start=True, stop=True),
                     reads=["rb_", "swps"], writes=["pb.7"])
                p.op("vector", lambda v: v.tensor_sub(cbf[0:64, :], ra[0:64, :], pb[7][0:64, :]),
                     reads=["ra", "pb.7"], writes=["cbf"])
                p.op("vector", lambda v: v.tensor_add(cbf[64:128, :], ra[64:128, :], pb[7][64:128, :]),
                     reads=["ra", "pb.7"], writes=["cbf"])
                p.op("vector", lambda v: v.tensor_copy(cbb[:, :], cbf[:, :]),
                     reads=["cbf"], writes=["cbb"])
                p.op("scalar", lambda s: s.activation(sqh[:, :], cbf[:, :], AF.Square),
                     reads=["cbf"], writes=["sqh"])
                def f_st(t):
                    t.matmul(pb[5][0:1, :], ones128b[:, :], cbb[:, :], start=True, stop=True)
                    return t.matmul(pb[6][0:1, :], ones128b[:, :], sqh[:, :], start=True, stop=True)
                p.op("tensor", f_st, reads=["cbb", "sqh", "ones128b"], writes=["pb.5", "pb.6"])
                p.op("vector", lambda v: v.tensor_copy(srow2[:, 0:512], pb[5][0:1, :]),
                     reads=["pb.5"], writes=["srow2"])
                p.op("vector", lambda v: v.tensor_copy(srow2[:, 512:1024], pb[6][0:1, :]),
                     reads=["pb.6"], writes=["srow2"])
                p.op("vector", lambda v: v.tensor_scalar_mul(srow2[:, :], srow2[:, :], 1.0 / 128),
                     reads=["srow2"], writes=["srow2"])
                p.op("vector", lambda v: v.tensor_mul(rrow2[:, :], srow2[:, 0:512], srow2[:, 0:512]),
                     reads=["srow2"], writes=["rrow2"])
                p.op("vector", lambda v: v.tensor_sub(rrow2[:, :], rrow2[:, :], srow2[:, 512:1024]),
                     reads=["srow2", "rrow2"], writes=["rrow2"])
                p.op("scalar", lambda s: s.activation(rrow2[:, :], rrow2[:, :], AF.Sqrt,
                                                      bias=consts[0:1, 0:1], scale=-1.0),
                     reads=["rrow2", "consts"], writes=["rrow2"])
                p.op("vector", lambda v: v.reciprocal(rrow2[:, :], rrow2[:, :]),
                     reads=["rrow2"], writes=["rrow2"])
                p.op("vector", lambda v: v.tensor_mul(mrrow2[:, :], srow2[:, 0:512], rrow2[:, :]),
                     reads=["srow2", "rrow2"], writes=["mrrow2"])
                def f_bc(t):
                    t.matmul(pb[5][:, :], onesrow[:, :], rrow2[0:1, :], start=True, stop=True)
                    return t.matmul(pb[6][:, :], onesrow[:, :], mrrow2[0:1, :], start=True, stop=True)
                p.op("tensor", f_bc, reads=["rrow2", "mrrow2", "onesrow"], writes=["pb.5", "pb.6"])
                p.op("vector", lambda v: v.tensor_copy(r2b[:, :], pb[5][:, :]),
                     reads=["pb.5"], writes=["r2b"])
                p.op("vector", lambda v: v.tensor_copy(mrb[:, :], pb[6][:, :]),
                     reads=["pb.6"], writes=["mrb"])
                p.op("vector", lambda v: v.tensor_mul(cbf[:, :], cbf[:, :], r2b[:, :]),
                     reads=["cbf", "r2b"], writes=["cbf"])
                p.op("vector", lambda v: v.tensor_sub(cbf[:, :], cbf[:, :], mrb[:, :]),
                     reads=["cbf", "mrb"], writes=["cbf"])
                p.op("vector", lambda v, dst=dst, nq=nq: v.tensor_scalar(
                        dst[:, nq * 512:(nq + 1) * 512], cbf[:, :],
                        lnss[:, 0:1], lnss[:, 1:2], ALU.mult, ALU.add),
                     reads=["cbf", "lnss"], writes=[f"{dk}.{nq}"])
        for qs in range(4):
            for kt in range(qs * 4 + 4):
                spar = kt % 2
                p.op("tensor", lambda t, hl=hl, qs=qs, kt=kt, spar=spar: t.matmul(
                        pb[spar][:, :], kb[:, kt * 128:(kt + 1) * 128],
                        qb[:, qs * 512:(qs + 1) * 512], start=True, stop=True),
                     reads=[f"kb.{kt // 4}", f"qb.{qs}"], writes=[f"pb.{spar}"])
                p.op("scalar", lambda s, spar=spar: s.activation(sa[:, :], pb[spar][:, :], AF.Square),
                     reads=[f"pb.{spar}"], writes=["sa"])
                if kt >= qs * 4:
                    off = kt - qs * 4
                    p.op("vector", lambda v, off=off: v.tensor_mul(
                            sa[:, :], sa[:, :], mask4s[:, off * 512:(off + 1) * 512]),
                         reads=["sa", "mask4s"], writes=["sa"])
                p.op("vector", lambda v: v.tensor_mul(a4b[:, :], sa[:, :], sa[:, :]),
                     reads=["sa"], writes=["a4b"])
                def f_acc(t, hl=hl, qs=qs, kt=kt):
                    t.matmul(pb[2][0:1, :], ones128b[:, :], a4b[:, :],
                             start=(kt == 0), stop=(kt == qs * 4 + 3))
                    return t.matmul(pb[3][:, :], v_all[:, (kt * 8 + hl) * 128:(kt * 8 + hl + 1) * 128],
                                    a4b[:, :], start=(kt == 0), stop=(kt == qs * 4 + 3))
                p.op("tensor", f_acc, reads=["a4b", "ones128b", f"v.{hl}.{kt}"],
                     writes=["pb.2", "pb.3"])
            p.op("vector", lambda v: v.tensor_copy(drow[:, :], pb[2][0:1, :]),
                 reads=["pb.2"], writes=["drow"])
            p.op("vector", lambda v: v.reciprocal(drow[:, :], drow[:, :]),
                 reads=["drow"], writes=["drow"])
            p.op("tensor", lambda t: t.matmul(pb[4][:, :], onesrow[:, :], drow[0:1, :],
                                              start=True, stop=True),
                 reads=["drow", "onesrow"], writes=["pb.4"])
            p.op("vector", lambda v: v.tensor_copy(rbq[:, :], pb[4][:, :]),
                 reads=["pb.4"], writes=["rbq"])
            p.op("vector", lambda v, qs=qs: v.tensor_mul(
                    attnb[:, qs * 512:(qs + 1) * 512], pb[3][:, :], rbq[:, :]),
                 reads=["pb.3", "rbq"], writes=[f"attnb.{qs}"])
        p.dma("sync", lambda s, hl=hl: s.dma_start(
            attn[hl * 128:(hl + 1) * 128, :], attnb[:, :]),
            reads=[f"attnb.{qs}" for qs in range(4)], writes=[f"attnout.{hl}"], sem="attnout")

    p.barrier_all("sync")
    p.finalize()
    ctx.close()
    return nc


def build_launch_E():
    nc = bass.Bass("TRN2", target_bir_lowering=False)
    attn_sl = nc.dram_tensor("attn_sl", [D, T], BF16, kind="ExternalInput").ap()
    w_out1 = nc.dram_tensor("w_out1", [4, DC, 128, 512], F32, kind="ExternalInput").ap()
    x1 = nc.dram_tensor("x1", [T, D], F32, kind="ExternalInput").ap()
    g1 = nc.dram_tensor("g1", [T, D], BF16, kind="ExternalInput").ap()
    x2 = nc.dram_tensor("x2", [T, D], F32, kind="ExternalOutput").ap()

    ctx = ExitStack()
    abuf = ctx.enter_context(nc.sbuf_tensor("abuf", [128, DC * T], BF16))
    xt = ctx.enter_context(nc.sbuf_tensor("xt", [128, TTI * D], F32))
    g1sl = [ctx.enter_context(nc.sbuf_tensor(f"g1sl{j}", [128, TTI * 512], BF16)) for j in range(2)]
    wsl = [ctx.enter_context(nc.sbuf_tensor(f"wsl{j}", [128, 8 * 512], BF16)) for j in range(2)]
    tmp = ctx.enter_context(nc.sbuf_tensor("tmp", [128, 512], F32))
    pb = [ctx.enter_context(nc.psum_tensor(f"pb{j}", [128, 512], F32)) for j in range(8)]

    p = Prog(nc, ctx)
    p.dma("gpsimd", lambda g: g.dma_start(
        abuf[:, :].rearrange("p (kc t) -> p kc t", t=T),
        attn_sl.rearrange("(kc p) t -> p kc t", p=128)),
        writes=[f"a.{kc}" for kc in range(DC)], sem="a")
    p.dma("gpsimd", lambda g: g.dma_start(
        xt[:, :].rearrange("p (tt d) -> p tt d", d=D),
        x1.rearrange("(tt p) d -> p tt d", p=128)),
        writes=[f"xt.{tt}" for tt in range(TTI)], sem="xt")

    for oc in range(4):
        gpar = oc % 2
        p.dma("gpsimd", lambda g, oc=oc, gpar=gpar: g.dma_start(
            g1sl[gpar][:, :].rearrange("p (tt o) -> p tt o", o=512),
            g1.rearrange("(tt p) d -> p tt d", p=128)[:, :, oc * 512:(oc + 1) * 512]),
            writes=[f"g1sl.{gpar}"], sem=f"g1sl{gpar}")
        for kcg in range(2):
            par = kcg % 2
            p.dma("gpsimd", lambda g, oc=oc, kcg=kcg, par=par: g.dma_start(
                wsl[par][:, :].rearrange("p (kc o) -> p kc o", o=512),
                w_out1[oc, kcg * 8:(kcg + 1) * 8].rearrange("kc k o -> k kc o")),
                writes=[f"wsl.{par}"], sem=f"wsl{par}")
            def f_mm(t, oc=oc, kcg=kcg, par=par):
                for kci in range(8):
                    kc = kcg * 8 + kci
                    for tt in range(TTI):
                        inst = t.matmul(
                            pb[tt][:, :],
                            abuf[:, kc * T + tt * 128: kc * T + (tt + 1) * 128],
                            wsl[par][:, kci * 512:(kci + 1) * 512],
                            start=(kc == 0), stop=(kc == DC - 1))
                return inst
            p.op("tensor", f_mm,
                 reads=[f"wsl.{par}"] + [f"a.{kcg * 8 + j}" for j in range(8)],
                 writes=[f"pb.{tt}" for tt in range(TTI)])
        for tt in range(TTI):
            p.op("vector", lambda v, oc=oc, tt=tt: v.tensor_mul(
                    tmp[:, :], pb[tt][:, :],
                    g1sl[oc % 2][:, tt * 512:(tt + 1) * 512]),
                 reads=[f"pb.{tt}", f"g1sl.{oc % 2}"], writes=["tmp"])
            p.op("vector", lambda v, oc=oc, tt=tt: v.tensor_add(
                    xt[:, tt * D + oc * 512: tt * D + (oc + 1) * 512],
                    xt[:, tt * D + oc * 512: tt * D + (oc + 1) * 512], tmp[:, :]),
                 reads=["tmp", f"xt.{tt}"], writes=[f"xt.{tt}"])
    for tt in range(TTI):
        p.dma("sync", lambda s, tt=tt: s.dma_start(
            x2[tt * 128:(tt + 1) * 128, :], xt[:, tt * D:(tt + 1) * D]),
            reads=[f"xt.{tt}"], writes=[f"x2out.{tt}"], sem="x2out")

    p.barrier_all("sync")
    p.finalize()
    ctx.close()
    return nc
